# version 07
# baseline (speedup 1.0000x reference)
import numpy as np
import ml_dtypes
import concourse.bass as bass
import concourse.mybir as mybir
from concourse.bass_utils import run_bass_kernel_spmd

F32 = mybir.dt.float32
BF16 = mybir.dt.bfloat16
U8 = mybir.dt.uint8
AF = mybir.ActivationFunctionType
ALU = mybir.AluOpType
AX = mybir.AxisListType

DM = 1024
DI = 2048
S = 128
Q = 128
NCH = 16
BLK = 8
T = 2048
TW = 2176
NT = TW // 128
J = 4353
JT = 34
WF = 12288
E = 8
WIDTHS = [1024] * 4 + [2048] * 4
OFFS = np.concatenate([[0], np.cumsum(WIDTHS)]).astype(int)
CLIP = 8.0
EPS = 1e-5
TBATCH = [(0, 512), (512, 512), (1024, 512), (1536, 512), (2048, 128)]
MM2G = [(0, 3), (3, 3), (6, 3), (9, 3), (12, 3), (15, 1)]

ENGINES = ("tensor", "vector", "scalar", "gpsimd", "sync")
STRICT_SAME_ENGINE = True
DEBUG_PHASE = None
DEBUG_P3B = 99
DEBUG_TAPS = False


class Prog:

    def __init__(self):
        self.steps = []
        self.cnt = {e: 0 for e in ENGINES}
        self.cnt["cc"] = 0
        self.lastw = {}
        self.readers = {}

    def step(self, engine, emit, reads=(), writes=(), dma=False, tag="",
             done_sem=None, extra_waits=()):
        deps = {}

        def dep(sem, val, asyn):
            if sem == engine and not asyn and not STRICT_SAME_ENGINE:
                return
            if deps.get(sem, 0) < val:
                deps[sem] = val

        for t in reads:
            w = self.lastw.get(t)
            if w:
                dep(*w)
        for t in writes:
            w = self.lastw.get(t)
            if w:
                dep(*w)
            for r in self.readers.get(t, ()):
                dep(*r)
        for sem, val in extra_waits:
            if deps.get(sem, 0) < val:
                deps[sem] = val
        if done_sem is not None:
            sem, amt = done_sem
        else:
            sem, amt = engine, (16 if dma else 1)
        self.cnt[sem] = self.cnt.get(sem, 0) + amt
        rec = (sem, self.cnt[sem], dma or done_sem is not None)
        for t in reads:
            self.readers.setdefault(t, []).append(rec)
        for t in writes:
            self.lastw[t] = rec
            self.readers[t] = []
        self.steps.append((engine, emit, deps, (sem, amt), tag))
        return rec

    def barrier(self):
        snapshot = dict(self.cnt)
        for e in ENGINES:
            waits = {s: v for s, v in snapshot.items() if v > 0 and s != e}
            self.steps.append((e, None, waits, (e, 0), "barrier"))
        self.lastw.clear()
        self.readers.clear()

    def final_wait(self, engine="gpsimd"):
        waits = {s: v for s, v in self.cnt.items() if v > 0}
        self.steps.append((engine, None, waits, (engine, 0), "final"))

    def emit(self, nc, block, sems):
        per = {e: [] for e in ENGINES}
        for stp in self.steps:
            per[stp[0]].append(stp)

        def mk(engine, steps):
            def body(eng):
                satisfied = {}
                for _, emitfn, deps, (sem, amt), tag in steps:
                    for s9, v in sorted(deps.items()):
                        if satisfied.get(s9, 0) >= v:
                            continue
                        eng.wait_ge(sems[s9], v)
                        satisfied[s9] = v
                    if emitfn is None:
                        continue
                    ins = emitfn(eng)
                    if amt > 0:
                        if ins is None:
                            raise RuntimeError(f"step {tag} returned no instruction")
                        ins.then_inc(sems[sem], amt)
            return body

        for e in ENGINES:
            if per[e]:
                getattr(block, e)(mk(e, per[e]))


def build_program():
    nc = bass.Bass("TRN2", target_bir_lowering=False)
    p = Prog()

    def inp(name, shape, dtype=F32):
        return nc.dram_tensor(name, shape, dtype, kind="ExternalInput").ap()

    xin = inp("xin", [TW, DM])
    WinT = inp("WinT", [DM, J])
    w1 = inp("w1", [DM, WF])
    w2 = inp("w2", [WF, DM])
    WoutT = inp("WoutT", [DI, DM])
    small9 = inp("small9", [DM, 9])
    A_log_row = inp("A_log_row", [1, S])
    cw_col = inp("cw_col", [128, 64])
    dp_col = inp("dp_col", [128, 16])
    nw_col = inp("nw_col", [128, 16])
    sel8 = inp("sel8", [128, 8])
    Mrows = inp("Mrows", [128, 512])
    maskoff = inp("maskoff", [128, 128])
    bandmask = inp("bandmask", [128, 8 * 128], U8)
    idf32 = inp("idf32", [128, 128])
    eyerow8 = inp("eyerow8", [8, 8 * 128])

    out = nc.dram_tensor("out", [T, DM], F32, kind="ExternalOutput").ap()
    dbg = {}
    if DEBUG_TAPS:
        for nm, shape, dt in [
                ("dts", [128, NT], F32), ("comb", [128, NT * 8], F32),
                ("s3a", [128, 2048], F32), ("bfm", [128, TW], BF16),
                ("cfm", [128, TW], BF16), ("xm0", [128, 2048], BF16),
                ("g0", [128, 1024], BF16), ("z0", [128, 2048], BF16),
                ("rows0", [1, 512], F32), ("W0", [128, 128], BF16),
                ("y0", [128, 2048], F32), ("rstd0", [128, 1], F32),
                ("abar0", [128, 128], F32), ("ck0", [1, 1024], F32),
                ("f2b0", [128, 128], F32), ("f10", [128, 128], F32),
                ("cs0", [128, 128], BF16), ("sr0", [128, 2048], F32),
                ("xnt0", [128, TW], BF16), ("hrow0", [128, 2048], BF16)]:
            dbg[nm] = nc.dram_tensor("dbg_" + nm, shape, dt,
                                     kind="ExternalOutput").ap()

    xnhi_scr = nc.dram_tensor("xnhi_scr", [TW, DM], BF16, kind="Internal").ap()
    xnlo_scr = nc.dram_tensor("xnlo_scr", [TW, DM], BF16, kind="Internal").ap()
    xnT_scr = nc.dram_tensor("xnT_scr", [128, 8 * TW], BF16, kind="Internal").ap()
    z_scr = nc.dram_tensor("z_scr", [DI, TW], BF16, kind="Internal").ap()
    H_scr = nc.dram_tensor("H_scr", [WF, T], BF16, kind="Internal").ap()
    g_scr = nc.dram_tensor("g_scr", [T, DM], BF16, kind="Internal").ap()
    ag_in = nc.dram_tensor("ag_in", [S, DI], F32, kind="Internal")
    ag_out = nc.dram_tensor("ag_out", [8 * S, DI], F32, kind="Internal")

    PB = [nc.alloc_psum_tensor(f"PB{i}", [128, 512], F32) for i in range(8)]
    BT = [f"B{i}" for i in range(8)]

    def sball(name, shape, dtype=F32):
        return nc.alloc_sbuf_tensor(name, shape, dtype)

    dts_tok = sball("dts_tok", [128, NT])
    combine = sball("combine", [128, NT * 8])
    A_row = sball("A_row", [1, 128])
    ones1 = sball("ones1", [1, 128])
    onesc = sball("onesc", [128, 1])
    rows_sb = sball("rows_sb", [1, 512])
    rQrow = sball("rQrow", [1, 128])
    rstdrow = sball("rstdrow", [1, 128])
    ck_sb = sball("ck_sb", [1, 1024])
    Mrows_sb = sball("Mrows_sb", [128, 512])
    maskoff_sb = sball("maskoff_sb", [128, 128])
    bandmask_sb = sball("bandmask_sb", [128, 8 * 128], U8)
    idf_sb = sball("idf_sb", [128, 128])
    eyerow8_sb = sball("eyerow8_sb", [8, 8 * 128])
    cw_sb = sball("cw_sb", [128, 64])
    dp_sb = sball("dp_sb", [128, 16])
    nw_sb = sball("nw_sb", [128, 16])
    sel8_sb = sball("sel8_sb", [128, 8])
    small_f = sball("small_f", [128, 8 * 9])
    small_hi = sball("small_hi", [128, 8 * 9], BF16)
    small_lo = sball("small_lo", [128, 8 * 9], BF16)
    smallhl = sball("smallhl", [128, 8 * 18], BF16)
    lnc = sball("lnc", [128, 16])
    headf = sball("headf", [128, 16])
    sqc_t = sball("sqc_t", [128, 8])
    dccol = sball("dccol", [128, 1])
    d8 = sball("d8", [128, 16])
    l8row = sball("l8row", [1, 16])
    srow_t = sball("srow_t", [1, 128])
    cbias = sball("cbias", [128, 2])

    arena_pp = 184 * 1024
    got = nc.bump_sbuf(arena_pp)
    assert got is not None, "arena alloc failed"
    abase = got[0]

    def at(name, off_pp, shape, dtype=F32):
        return nc.alloc_sbuf_tensor_at(name, shape, dtype, offset=abase + off_pp)

    K = 1024
    xnT_hi = at("xnT_hi", 0, [128, 8 * TW], BF16)
    w1buf = [at("w1buf0", 0, [128, 8 * 512], BF16),
             at("w1buf1", 8 * K, [128, 8 * 512], BF16)]
    Hrelu = at("Hrelu", 16 * K, [128, 2048], BF16)
    Hsqb = [at("Hsqb0", 20 * K, [128, 2048], BF16),
            at("Hsqb1", 24 * K, [128, 2048], BF16)]
    w2buf = [at("w2buf0", 0, [128, 8 * 1024], BF16),
             at("w2buf1", 16 * K, [128, 8 * 1024], BF16)]
    WoutT_sb = at("WoutT_sb", 0, [128, 16 * 1024], BF16)

    B0 = 34 * K
    xnT_lo = at("xnT_lo", B0, [128, 8 * TW], BF16)
    prebuf = [at("prebuf0", B0, [128, TW], BF16),
              at("prebuf1", B0 + 4352, [128, TW], BF16)]
    wbuf = [at("wbuf0", B0 + 8704, [128, 8 * 128], BF16),
            at("wbuf1", B0 + 10752, [128, 8 * 128], BF16)]
    zbuf = [at("zbuf0", B0 + 12800, [128, 512], BF16),
            at("zbuf1", B0 + 13824, [128, 512], BF16)]
    cvt0 = at("cvt0", B0 + 14848, [128, 512])
    cvt1 = at("cvt1", B0 + 17 * K, [128, 1024])
    cvt2 = at("cvt2", B0 + 21 * K, [128, 1024])
    cvacc = at("cvacc", B0 + 25 * K, [128, 2048])
    xnTs = [at("xnTs0", B0, [128, 2048], BF16),
            at("xnTs1", B0 + 4 * K, [128, 2048], BF16)]
    sq_fm = at("sq_fm", B0 + 8 * K, [8, 2048])
    sqb_sb = at("sqb_sb", B0 + 16 * K, [128, 2048], BF16)
    hbuf = [at("hbuf0", B0, [128, 8 * 384], BF16),
            at("hbuf1", B0 + 6 * K, [128, 8 * 384], BF16)]
    gout = [at("gout0", B0 + 12 * K, [128, 1024], BF16),
            at("gout1", B0 + 14 * K, [128, 1024], BF16)]
    zc = [at("zc0", B0, [128, 2048], BF16),
          at("zc1", B0 + 4 * K, [128, 2048], BF16)]
    gbuf = [at("gbuf0", B0 + 8 * K, [128, 1024], BF16),
            at("gbuf1", B0 + 10 * K, [128, 1024], BF16)]
    xres = at("xres", B0 + 12 * K, [128, 1024])
    outbuf = [at("outbuf0", B0 + 16 * K, [128, 1024]),
              at("outbuf1", B0 + 20 * K, [128, 1024])]
    y_fm = at("y_fm", B0 + 24 * K, [128, 2048])

    C0 = 68 * K
    xm_fm = at("xm_fm", C0, [128, 16 * 2048], BF16)

    D0 = 132 * K
    S_run = at("S_run", D0, [128, 2048])
    B_fm = at("B_fm", D0 + 8 * K, [128, TW], BF16)
    C_fm = at("C_fm", D0 + 8 * K + 4352, [128, TW], BF16)
    DS = D0 + 17 * K
    xmch_f = at("xmch_f", DS, [128, 2048])
    xm_tok = at("xm_tok", DS + 8 * K, [128, 2048], BF16)
    xnT_e = at("xnT_e", DS, [128, 8 * 2048], BF16)
    M0 = DS + 12 * K
    dtsb = at("dtsb", M0, [128, 128])
    ebuf = at("ebuf", M0 + 512, [128, 128])
    F1 = at("F1", M0 + 1024, [128, 128])
    F2base = at("F2base", M0 + 1536, [128, 128])
    F2run = at("F2run", M0 + 2048, [128, 128])
    Btld = at("Btld", M0 + 2560, [128, 128])
    Bend = at("Bend", M0 + 3072, [128, 128])
    Bendtok = at("Bendtok", M0 + 3584, [128, 128], BF16)
    Cs = at("Cs", M0 + 3840, [128, 128], BF16)
    A_bar = at("A_bar", M0 + 4096, [128, 128])
    Ppp = [at("Ppp0", M0 + 4608, [128, 128]),
           at("Ppp1", M0 + 5120, [128, 128])]
    zcw = at("zcw", M0 + 5632, [128, 128])
    Wsb = at("Wsb", M0 + 6144, [128, 128], BF16)
    ckb = at("ckb", M0 + 6400, [128, 1024], BF16)
    Zbuf = at("Zbuf", M0 + 8448, [128, 1024])
    S_run_bf = at("S_run_bf", M0 + 12544, [128, 2048], BF16)
    yg = at("yg", M0 + 16640, [128, 2048], BF16)
    sgbuf = at("sgbuf", M0 + 16640, [128, 1024])
    rstd_col = at("rstd_col", M0 + 20736, [128, 1])
    xq = [at("xq0", DS, [128, 1024]), at("xq1", DS + 4 * K, [128, 1024])]
    lnjunk = at("lnjunk", DS + 8 * K, [128, 1024])
    xnf = at("xnf", DS + 12 * K, [128, 1024])
    xnhib = at("xnhib", DS + 16 * K, [128, 1024], BF16)
    xnlob = at("xnlob", DS + 18 * K, [128, 1024], BF16)

    def dma(queue, o, i, reads, writes, tag="", **kw):
        return p.step(queue, lambda e, o=o, i=i, kw=kw: e.dma_start(out=o, in_=i, **kw),
                      reads=reads, writes=writes, dma=True, tag=tag)

    def dmat(o, i, reads, writes, tag=""):
        return p.step("scalar", lambda e, o=o, i=i: e.dma_start_transpose(o, i),
                      reads=reads, writes=writes, dma=True, tag=tag)

    def st(engine, fn, reads=(), writes=(), tag="", extra_waits=()):
        return p.step(engine, fn, reads=reads, writes=writes, tag=tag,
                      extra_waits=extra_waits)

    def tap(nm, src_ap, reads):
        if DEBUG_TAPS and nm in dbg:
            dma("sync", dbg[nm], src_ap, reads, [f"_tap_{nm}"], tag=f"tap_{nm}")

    dma("sync", Mrows_sb[:, :], Mrows, [], ["Mrows"])
    dma("sync", maskoff_sb[:, :], maskoff, [], ["maskoff"])
    dma("sync", bandmask_sb[:, :], bandmask, [], ["bandmask"])
    dma("sync", idf_sb[:, :], idf32, [], ["idf"])
    dma("sync", eyerow8_sb[:, :], eyerow8, [], ["eyerow8"])
    dma("sync", cw_sb[:, :], cw_col, [], ["cw"])
    dma("sync", dp_sb[:, :], dp_col, [], ["dp"])
    dma("sync", nw_sb[:, :], nw_col, [], ["nw"])
    dma("sync", sel8_sb[:, :], sel8, [], ["sel8"])
    dma("sync", small_f[:, :].rearrange("p (s e) -> p s e", s=8),
        small9.rearrange("(s p) e -> p s e", p=128),
        [], ["small_f"], allow_slow_non_contiguous=True)
    dma("sync", srow_t[:, :], A_log_row, [], ["alog"])
    st("vector", lambda e: e.memset(cbias[:, 0:1], EPS), [], ["cbias"])
    st("vector", lambda e: e.memset(cbias[:, 1:2], 1.0), [], ["cbias"])
    st("vector", lambda e: e.memset(ones1[:, :], 1.0), [], ["ones1"])
    st("vector", lambda e: e.memset(onesc[:, :], 1.0), [], ["onesc"])
    st("vector", lambda e: e.memset(F2run[:, :], 0.0), [], ["F2run"])
    st("vector", lambda e: e.memset(Zbuf[:, :], 0.0), [], ["Zbuf"])
    st("vector", lambda e: e.memset(Ppp[0][:, :], 1.0), [], ["Ppp0"])
    st("vector", lambda e: e.memset(Ppp[1][:, :], 1.0), [], ["Ppp1"])
    st("scalar", lambda e: e.activation(rstdrow[:, :], srow_t[:, :], AF.Exp),
       ["alog"], ["rstdrow"])
    st("vector", lambda e: e.tensor_scalar_mul(A_row[:, :], rstdrow[:, :], -1.0),
       ["rstdrow"], ["A_row"])
    st("vector", lambda e: e.tensor_copy(small_hi[:, :], small_f[:, :]),
       ["small_f"], ["small_hi"])
    st("vector", lambda e: e.tensor_sub(small_lo[:, :], small_f[:, :], small_hi[:, :]),
       ["small_f", "small_hi"], ["small_lo"])
    for s8 in range(8):
        st("vector", lambda e, s8=s8: e.tensor_copy(
            smallhl[:, s8 * 18:s8 * 18 + 9], small_hi[:, s8 * 9:(s8 + 1) * 9]),
           ["small_hi"], ["smallhl"])
        st("vector", lambda e, s8=s8: e.tensor_copy(
            smallhl[:, s8 * 18 + 9:s8 * 18 + 18], small_lo[:, s8 * 9:(s8 + 1) * 9]),
           ["small_lo"], ["smallhl"])

    for i in range(NT):
        xb = xq[i % 2]
        xtok = f"xq{i % 2}"
        dma("sync", xb[:, :], xin[i * 128:(i + 1) * 128, :], [], [xtok], tag=f"ldx{i}")
        st("scalar", lambda e, xb=xb: e.activation(
            lnjunk[:, :], xb[:, :], AF.Square, accum_out=lnc[:, 0:1]),
           [xtok], ["lnjunk", "lnc"], tag=f"ssq{i}")
        st("scalar", lambda e, xb=xb: e.activation(
            lnjunk[:, :], xb[:, :], AF.Copy, accum_out=lnc[:, 1:2]),
           [xtok], ["lnjunk", "lnc"], tag=f"msum{i}")
        st("vector", lambda e: e.tensor_scalar_mul(lnc[:, 2:3], lnc[:, 1:2], 1.0 / DM),
           ["lnc"], ["lnc"])
        st("vector", lambda e: e.tensor_scalar_mul(lnc[:, 3:4], lnc[:, 0:1], 1.0 / DM),
           ["lnc"], ["lnc"])
        st("vector", lambda e: e.scalar_tensor_tensor(
            lnc[:, 4:5], lnc[:, 2:3], lnc[:, 2:3], lnc[:, 3:4], ALU.mult, ALU.subtract),
           ["lnc"], ["lnc"])
        st("scalar", lambda e: e.activation(lnc[:, 5:6], lnc[:, 4:5], AF.Ln,
                                            scale=-1.0, bias=cbias[:, 0:1]),
           ["lnc", "cbias"], ["lnc"])
        st("scalar", lambda e: e.activation(lnc[:, 6:7], lnc[:, 5:6], AF.Exp, scale=-0.5),
           ["lnc"], ["lnc"])
        st("vector", lambda e, xb=xb: e.tensor_scalar(
            xnhib[:, :], xb[:, :], lnc[:, 2:3], lnc[:, 6:7], ALU.subtract, ALU.mult),
           [xtok, "lnc"], ["xnhib"], tag=f"xnhi{i}")
        st("vector", lambda e, xb=xb: e.tensor_scalar(
            xnf[:, :], xb[:, :], lnc[:, 2:3], lnc[:, 6:7], ALU.subtract, ALU.mult),
           [xtok, "lnc"], ["xnf"], tag=f"xnf{i}")
        st("vector", lambda e: e.tensor_sub(xnlob[:, :], xnf[:, :], xnhib[:, :]),
           ["xnf", "xnhib"], ["xnlob"], tag=f"xnlo{i}")
        dma("sync", xnhi_scr[i * 128:(i + 1) * 128, :], xnhib[:, :],
            ["xnhib"], [f"xnhis{i}"], tag=f"sthi{i}")
        dma("sync", xnlo_scr[i * 128:(i + 1) * 128, :], xnlob[:, :],
            ["xnlob"], [f"xnlos{i}"], tag=f"stlo{i}")

    all_hi = [f"xnhis{i}" for i in range(NT)]
    all_lo = [f"xnlos{i}" for i in range(NT)]
    for sl in range(8):
        dmat(xnT_hi[:, sl * TW:(sl + 1) * TW],
             xnhi_scr[:, sl * 128:(sl + 1) * 128],
             all_hi, [f"xnT_hi_{sl}"], tag=f"trh{sl}")
        dmat(xnT_lo[:, sl * TW:(sl + 1) * TW],
             xnlo_scr[:, sl * 128:(sl + 1) * 128],
             all_lo, [f"xnT_lo_{sl}"], tag=f"trl{sl}")
    dma("sync", xnT_scr, xnT_hi[:, :],
        [f"xnT_hi_{sl}" for sl in range(8)], ["xnT_scr"], tag="xnTscr")

    for i in range(1, NT):
        def head_mm(e, i=i):
            last = None
            for dm8 in range(8):
                lh = xnT_hi[:, dm8 * TW + i * 128: dm8 * TW + (i + 1) * 128]
                last = e.matmul(PB[6][:, 0:18], lh, smallhl[:, dm8 * 18:(dm8 + 1) * 18],
                                start=(dm8 == 0), stop=False)
            for dm8 in range(8):
                ll = xnT_lo[:, dm8 * TW + i * 128: dm8 * TW + (i + 1) * 128]
                last = e.matmul(PB[6][:, 0:9], ll, small_hi[:, dm8 * 9:(dm8 + 1) * 9],
                                start=False, stop=(dm8 == 7))
            return last
        st("tensor", head_mm,
           [f"xnT_hi_{d}" for d in range(8)] + [f"xnT_lo_{d}" for d in range(8)]
           + ["smallhl", "small_hi"], ["B6"], tag=f"headmm{i}")
        st("vector", lambda e: e.tensor_copy(lnjunk[:, 0:18], PB[6][:, 0:18]),
           ["B6"], ["lnjunk"], tag=f"headcp{i}")
        st("vector", lambda e: e.tensor_add(headf[:, 0:9], lnjunk[:, 0:9],
                                            lnjunk[:, 9:18]),
           ["lnjunk"], ["headf"], tag=f"headadd{i}")
        st("scalar", lambda e: e.activation(headf[:, 9:10], headf[:, 8:9], AF.Abs),
           ["headf"], ["headf"])
        st("scalar", lambda e: e.activation(headf[:, 10:11], headf[:, 9:10], AF.Exp,
                                            scale=-1.0),
           ["headf"], ["headf"])
        st("scalar", lambda e: e.activation(headf[:, 11:12], headf[:, 10:11], AF.Ln,
                                            bias=cbias[:, 1:2]),
           ["headf", "cbias"], ["headf"])
        st("vector", lambda e: e.tensor_scalar_max(headf[:, 12:13], headf[:, 8:9], 0.0),
           ["headf"], ["headf"])
        st("vector", lambda e, i=i: e.tensor_add(
            dts_tok[:, i:i + 1], headf[:, 11:12], headf[:, 12:13]),
           ["headf"], ["dts_tok"], tag=f"dts{i}")
        csl = combine[:, i * 8:(i + 1) * 8]
        st("scalar", lambda e, csl=csl: e.activation(csl, headf[:, 0:8], AF.Sigmoid),
           ["headf"], ["probs"], tag=f"sig{i}")
        st("vector", lambda e, csl=csl: e.reduce_max(headf[:, 13:14], csl, AX.X),
           ["probs"], ["headf"])
        st("vector", lambda e, csl=csl: e.tensor_scalar(
            lnjunk[:, 0:8], csl, headf[:, 13:14], None, ALU.is_equal),
           ["probs", "headf"], ["lnjunk"])
        st("vector", lambda e, csl=csl: e.scalar_tensor_tensor(
            lnjunk[:, 8:16], lnjunk[:, 0:8], -1e30, csl, ALU.mult, ALU.add),
           ["lnjunk", "probs"], ["lnjunk"])
        st("vector", lambda e: e.reduce_max(headf[:, 14:15], lnjunk[:, 8:16], AX.X),
           ["lnjunk"], ["headf"])
        st("vector", lambda e: e.scalar_tensor_tensor(
            headf[:, 15:16], headf[:, 13:14], 1.0, headf[:, 14:15], ALU.mult, ALU.add),
           ["headf"], ["headf"])
        st("vector", lambda e: e.tensor_scalar_add(headf[:, 15:16], headf[:, 15:16], 1e-20),
           ["headf"], ["headf"])
        st("vector", lambda e: e.reciprocal(lnc[:, 8:9], headf[:, 15:16]),
           ["headf"], ["lnc2"])
        st("vector", lambda e, csl=csl: e.tensor_scalar(
            lnjunk[:, 0:8], csl, headf[:, 14:15], None, ALU.is_ge),
           ["probs", "headf"], ["lnjunk"])
        st("vector", lambda e, csl=csl: e.tensor_mul(csl, csl, lnjunk[:, 0:8]),
           ["probs", "lnjunk"], ["probs"])
        st("vector", lambda e, csl=csl: e.tensor_scalar_mul(csl, csl, lnc[:, 8:9]),
           ["probs", "lnc2"], ["probs"], tag=f"comb{i}")

    tap("dts", dts_tok[:, :], ["dts_tok"])
    tap("comb", combine[:, :], ["probs"])
    tap("xnt0", xnT_hi[:, 0:TW], ["xnT_hi_0"])

    p.barrier()
    if DEBUG_PHASE == "p1":
        p.final_wait()
        return nc, p

    st("vector", lambda e: e.memset(S_run[:, :], 0.0), [], ["S_run"])

    def evac_z(jt, tb_i, t0, tn):
        zb = zbuf[tb_i % 2]
        ztok = f"zbuf{tb_i % 2}"
        bank = BT[tb_i % 2]
        st("scalar", lambda e, tn=tn, tb_i=tb_i: e.activation(
            cvt0[:, 0:tn], PB[tb_i % 2][:, 0:tn], AF.Silu),
           [bank], ["cvt0"], tag=f"zsilu{jt}_{tb_i}")
        st("vector", lambda e, tn=tn, zb=zb: e.tensor_scalar_min(
            zb[:, 0:tn], cvt0[:, 0:tn], CLIP),
           ["cvt0"], [ztok])
        dma("sync", z_scr[jt * 128:(jt + 1) * 128, t0:t0 + tn], zb[:, 0:tn],
            [ztok], [f"zscr{jt}_{tb_i}"], tag=f"zst{jt}_{tb_i}")

    def evac_xm(jt, tb_i, t0, tn):
        sl = jt - 16
        pb = prebuf[sl % 2]
        ptok = f"prebuf{sl % 2}"
        bank = BT[tb_i % 2]
        st("scalar", lambda e, tn=tn, tb_i=tb_i: e.activation(
            cvt1[:, 0:tn], PB[tb_i % 2][:, 0:tn], AF.Silu),
           [bank], ["cvt1"], tag=f"xmsilu{jt}_{tb_i}")
        st("vector", lambda e, t0=t0, tn=tn, pb=pb: e.tensor_scalar_min(
            pb[:, t0:t0 + tn], cvt1[:, 0:tn], CLIP),
           ["cvt1"], [ptok])

    def evac_bc(jt, tb_i, t0, tn):
        dst = B_fm if jt == 32 else C_fm
        tok = "B_fm" if jt == 32 else "C_fm"
        st("vector", lambda e, dst=dst, t0=t0, tn=tn, tb_i=tb_i: e.tensor_copy(
            dst[:, t0:t0 + tn], PB[tb_i % 2][:, 0:tn]),
           [BT[tb_i % 2]], [tok], tag=f"bc{jt}_{tb_i}")

    def conv_slot(sl):
        pb = prebuf[sl % 2]
        ptok = f"prebuf{sl % 2}"
        eng = "vector"
        acc = cvacc if sl % 2 else xmch_f
        atok = "cvacc" if sl % 2 else "xmch_f"
        st(eng, lambda e, sl=sl, pb=pb, acc=acc: e.tensor_scalar(
            acc[:, :], pb[:, 125:125 + T], cw_sb[:, sl * 4:sl * 4 + 1], None, ALU.mult),
           [ptok, "cw"], [atok], tag=f"cv0_{sl}")
        for k4 in range(1, 4):
            st(eng, lambda e, sl=sl, k4=k4, pb=pb, acc=acc: e.scalar_tensor_tensor(
                acc[:, :], pb[:, 125 + k4:125 + k4 + T],
                cw_sb[:, sl * 4 + k4:sl * 4 + k4 + 1], acc[:, :], ALU.mult, ALU.add),
               [ptok, "cw", atok], [atok], tag=f"cv{k4}_{sl}")
        for hh in range(2):
            cv = cvt1 if hh == 0 else cvt2
            cvtok = "cvt1" if hh == 0 else "cvt2"
            st("scalar", lambda e, acc=acc, hh=hh, cv=cv: e.activation(
                cv[:, :], acc[:, hh * 1024:(hh + 1) * 1024], AF.Silu),
               [atok], [cvtok], tag=f"cs{hh}_{sl}")
            st("vector", lambda e, sl=sl, hh=hh, cv=cv: e.tensor_scalar_min(
                xm_fm[:, sl * 2048 + hh * 1024: sl * 2048 + (hh + 1) * 1024],
                cv[:, :], CLIP),
               [cvtok], [f"xm{sl}"], tag=f"cm{hh}_{sl}")

    for jt in range(JT):
        wb = wbuf[jt % 2]
        wtok = f"wbuf{jt % 2}"
        dma("gpsimd", wb[:, :].rearrange("p (s j) -> p s j", s=8),
            WinT.rearrange("(s p) j -> p s j", p=128)[:, :, jt * 128:(jt + 1) * 128],
            [], [wtok], tag=f"ldw{jt}")
        for tb_i, (t0, tn) in enumerate(TBATCH):
            bank = BT[tb_i % 2]
            def mmj(e, wb=wb, t0=t0, tn=tn, tb_i=tb_i):
                last = None
                for dm8 in range(8):
                    last = e.matmul(
                        PB[tb_i % 2][:, 0:tn],
                        wb[:, dm8 * 128:(dm8 + 1) * 128],
                        xnT_hi[:, dm8 * TW + t0: dm8 * TW + t0 + tn],
                        start=(dm8 == 0), stop=(dm8 == 7))
                return last
            st("tensor", mmj, [wtok], [bank], tag=f"ip{jt}_{tb_i}")
            if jt < 16:
                evac_z(jt, tb_i, t0, tn)
            elif jt < 32:
                evac_xm(jt, tb_i, t0, tn)
            else:
                evac_bc(jt, tb_i, t0, tn)
        if 16 <= jt < 32:
            conv_slot(jt - 16)

    def chunk_rows(c):
        st("tensor", lambda e, c=c: e.matmul(
            PB[6][0:1, 0:512], dts_tok[:, c + 1:c + 2], Mrows_sb[:, :],
            start=True, stop=True),
           ["dts_tok", "Mrows"], ["B6"], tag=f"rows{c}")
        st("vector", lambda e: e.tensor_copy(rows_sb[:, :], PB[6][0:1, 0:512]),
           ["B6"], ["rows"], tag=f"rowsev{c}")
        st("vector", lambda e: e.tensor_scalar(
            rQrow[:, :], rows_sb[:, 384:512], -1.0, rows_sb[0:1, 511:512],
            ALU.mult, ALU.add),
           ["rows"], ["rQ"], tag=f"rq{c}")
        st("vector", lambda e: e.tensor_copy(l8row[:, :], rows_sb[0:1, 135:256:8]),
           ["rows"], ["l8"], tag=f"l8{c}")
        st("tensor", lambda e: e.matmul(PB[7][:, 0:128], ones1[:, :],
                                        rows_sb[0:1, 0:128], start=True, stop=True),
           ["rows", "ones1"], ["B7"], tag=f"dtsbmm{c}")
        st("vector", lambda e: e.tensor_copy(dtsb[:, :], PB[7][:, 0:128]),
           ["B7"], ["dtsb"], tag=f"dtsbev{c}")
        st("tensor", lambda e: e.matmul(PB[7][:, 128:144], A_row[:, :], l8row[:, :],
                                        start=True, stop=True),
           ["l8", "A_row"], ["B7"], tag=f"d8mm{c}")
        st("scalar", lambda e: e.activation(d8[:, :], PB[7][:, 128:144], AF.Exp),
           ["B7"], ["d8"], tag=f"d8e{c}")
        st("tensor", lambda e: e.matmul(PB[7][:, 144:145], A_row[:, :],
                                        rows_sb[0:1, 511:512], start=True, stop=True),
           ["rows", "A_row"], ["B7"], tag=f"dcmm{c}")
        st("scalar", lambda e: e.activation(dccol[:, :], PB[7][:, 144:145], AF.Exp),
           ["B7"], ["dc"], tag=f"dce{c}")

    def chunk_exp(c, rowap, rowtok, outsb, outtok, tag):
        st("tensor", lambda e, rowap=rowap: e.matmul(
            PB[7][:, 256:384], A_row[:, :], rowap, start=True, stop=True),
           [rowtok, "A_row"], ["B7"], tag=f"{tag}mm{c}")
        st("scalar", lambda e, outsb=outsb: e.activation(
            outsb[:, :], PB[7][:, 256:384], AF.Exp),
           ["B7"], [outtok], tag=f"{tag}e{c}")

    def chunk_btld(c):
        cc0 = 128 + c * 128
        st("vector", lambda e, cc0=cc0: e.tensor_mul(
            Btld[:, :], B_fm[:, cc0:cc0 + 128], dtsb[:, :]),
           ["B_fm", "dtsb"], ["Btld"], tag=f"btld{c}")

    def chunk_bend(c):
        chunk_exp(c, rQrow[:, :], "rQ", ebuf, "ebuf", "erq")
        st("vector", lambda e: e.tensor_mul(Bend[:, :], Btld[:, :], ebuf[:, :]),
           ["Btld", "ebuf"], ["Bend"], tag=f"bend{c}")
        st("tensor", lambda e: e.transpose(PB[7][:, 384:512], Bend[:, :], idf_sb[:, :]),
           ["Bend", "idf"], ["B7"], tag=f"bendtr{c}")
        st("vector", lambda e: e.tensor_copy(Bendtok[:, :], PB[7][:, 384:512]),
           ["B7"], ["Bendtok"], tag=f"bendev{c}")

    def chunk_xmtok(c):
        rc0 = c * 128
        st("vector", lambda e, rc0=rc0: e.tensor_copy(
            xmch_f[:, :].rearrange("p (s t) -> p s t", s=16),
            xm_fm[:, :].rearrange("p (s t) -> p s t", s=16)[:, :, rc0:rc0 + 128]),
           [f"xm{sl}" for sl in range(16)], ["xmch_f"], tag=f"xmf{c}")
        for d16 in range(16):
            bank = 6 + (d16 % 2)
            st("tensor", lambda e, d16=d16, bank=bank: e.transpose(
                PB[bank][:, 128:256], xmch_f[:, d16 * 128:(d16 + 1) * 128], idf_sb[:, :]),
               ["xmch_f", "idf"], [BT[bank]], tag=f"xtr{c}_{d16}")
            st("vector", lambda e, d16=d16, bank=bank: e.tensor_copy(
                xm_tok[:, d16 * 128:(d16 + 1) * 128], PB[bank][:, 128:256]),
               [BT[bank]], ["xm_tok"], tag=f"xtrev{c}_{d16}")

    def chunk_state(c):
        def smm(e):
            last = None
            for q4 in range(4):
                last = e.matmul(PB[q4][:, :], Bendtok[:, :],
                                xm_tok[:, q4 * 512:(q4 + 1) * 512],
                                start=True, stop=True)
            return last
        st("tensor", smm, ["Bendtok", "xm_tok"], BT[0:4], tag=f"smm{c}")
        for q4 in range(4):
            st("vector", lambda e, q4=q4: e.scalar_tensor_tensor(
                S_run[:, q4 * 512:(q4 + 1) * 512], S_run[:, q4 * 512:(q4 + 1) * 512],
                dccol[:, :], PB[q4][:, :], ALU.mult, ALU.add),
               ["S_run", "dc", BT[q4]], ["S_run"], tag=f"sup{c}_{q4}")

    for c in range(NCH):
        chunk_rows(c)
        chunk_btld(c)
        chunk_bend(c)
        chunk_xmtok(c)
        chunk_state(c)

    tap("s3a", S_run[:, :], ["S_run"])
    tap("bfm", B_fm[:, :], ["B_fm"])
    tap("cfm", C_fm[:, :], ["C_fm"])
    tap("xm0", xm_fm[:, 0:2048], ["xm0"])
    dma("sync", ag_in.ap(), S_run[:, :], ["S_run"], ["ag_in"], tag="agst")
    p.step("gpsimd", lambda e: e.collective_compute(
        "AllGather", ALU.bypass, replica_groups=[list(range(8))],
        ins=[ag_in.ap().opt()], outs=[ag_out.ap().opt()]),
        reads=["ag_in"], writes=["ag_out"], done_sem=("cc", 1), tag="allgather")

    p.barrier()
    if DEBUG_PHASE == "p3a":
        p.final_wait()
        return nc, p

    for i in range(16):
        st("scalar", lambda e, i=i: e.activation(
            sqc_t[:, :], combine[:, (i + 1) * 8:(i + 2) * 8], AF.Sqrt),
           ["combine"], ["sqc"], tag=f"sqrt{i}")
        st("tensor", lambda e: e.transpose(PB[6][0:8, 0:128], sqc_t[:, :], idf_sb[:, :]),
           ["sqc", "idf"], ["B6"], tag=f"sqtr{i}")
        st("vector", lambda e, i=i: e.tensor_copy(
            sq_fm[0:8, i * 128:(i + 1) * 128], PB[6][0:8, 0:128]),
           ["B6"], ["sq_fm"], tag=f"sqev{i}")

    wti = 0
    for e8 in range(E):
        we = WIDTHS[e8]
        def sqbmm(e, e8=e8):
            last = None
            for q4 in range(4):
                last = e.matmul(PB[q4][:, :], eyerow8_sb[:, e8 * 128:(e8 + 1) * 128],
                                sq_fm[0:8, q4 * 512:(q4 + 1) * 512],
                                start=True, stop=True)
            return last
        st("tensor", sqbmm, ["sq_fm", "eyerow8"], BT[0:4], tag=f"sqb{e8}")
        for q4 in range(4):
            st("vector", lambda e, q4=q4: e.tensor_copy(
                sqb_sb[:, q4 * 512:(q4 + 1) * 512], PB[q4][:, :]),
               [BT[q4]], ["sqb"], tag=f"sqbev{e8}_{q4}")
        for dm8 in range(8):
            xs = xnTs[dm8 % 2]
            xst = f"xnTs{dm8 % 2}"
            dma("sync", xs[:, :], xnT_scr[:, dm8 * TW + 128: dm8 * TW + TW],
                [], [xst], tag=f"ldxnT{e8}_{dm8}")
            st("vector", lambda e, dm8=dm8, xs=xs: e.tensor_mul(
                xnT_e[:, dm8 * 2048:(dm8 + 1) * 2048], xs[:, :], sqb_sb[:, :]),
               [xst, "sqb"], [f"xnT_e{dm8}"], tag=f"xe{e8}_{dm8}")
        for ch in range(we // 512):
            cw0 = int(OFFS[e8]) + ch * 512
            wb1 = w1buf[ch % 2]
            wt1 = f"w1buf{ch % 2}"
            dma("gpsimd", wb1[:, :].rearrange("p (s w) -> p s w", s=8),
                w1.rearrange("(s p) w -> p s w", p=128)[:, :, cw0:cw0 + 512],
                [], [wt1], tag=f"ldw1_{e8}_{ch}")
            for wt4 in range(4):
                pbase = (wti % 2) * 4
                def mm1(e, wb1=wb1, wt4=wt4, pbase=pbase):
                    last = None
                    for q4 in range(4):
                        for dm8 in range(8):
                            last = e.matmul(
                                PB[pbase + q4][:, :],
                                wb1[:, dm8 * 512 + wt4 * 128: dm8 * 512 + (wt4 + 1) * 128],
                                xnT_e[:, dm8 * 2048 + q4 * 512: dm8 * 2048 + (q4 + 1) * 512],
                                start=(dm8 == 0), stop=(dm8 == 7))
                    return last
                st("tensor", mm1, [wt1] + [f"xnT_e{d}" for d in range(8)],
                   [BT[pbase + q] for q in range(4)], tag=f"mm1_{wti}")
                hb = Hsqb[wti % 2]
                htok = f"Hsqb{wti % 2}"
                for q4 in range(4):
                    st("scalar", lambda e, pbase=pbase, q4=q4: e.activation(
                        Hrelu[:, q4 * 512:(q4 + 1) * 512], PB[pbase + q4][:, :], AF.Relu),
                       [BT[pbase + q4]], ["Hrelu"], tag=f"rel{wti}_{q4}")
                    st("vector", lambda e, q4=q4, hb=hb: e.tensor_mul(
                        hb[:, q4 * 512:(q4 + 1) * 512],
                        Hrelu[:, q4 * 512:(q4 + 1) * 512],
                        Hrelu[:, q4 * 512:(q4 + 1) * 512]),
                       ["Hrelu"], [htok], tag=f"hsq{wti}_{q4}")
                dma("sync", H_scr[wti * 128:(wti + 1) * 128, :], hb[:, :],
                    [htok], [f"Hrow{wti}"], tag=f"sth{wti}")
                wti += 1
    assert wti == 96

    p.barrier()

    for gi, (g0, gn) in enumerate(MM2G):
        t0 = g0 * 128
        tn = gn * 128
        for wb8 in range(12):
            hb2 = hbuf[wb8 % 2]
            ht2 = f"hbuf{wb8 % 2}"
            wb2 = w2buf[wb8 % 2]
            wt2 = f"w2buf{wb8 % 2}"
            dma("sync", hb2[:, 0:8 * tn].rearrange("p (s t) -> p s t", s=8),
                H_scr[wb8 * 1024:(wb8 + 1) * 1024, t0:t0 + tn].rearrange(
                    "(s p) t -> p s t", p=128),
                [f"Hrow{w}" for w in range(wb8 * 8, wb8 * 8 + 8)], [ht2],
                tag=f"ldh{gi}_{wb8}")
            dma("gpsimd", wb2[:, :].rearrange("p (s d) -> p s d", s=8),
                w2[wb8 * 1024:(wb8 + 1) * 1024, :].rearrange("(s p) d -> p s d", p=128),
                [], [wt2], tag=f"ldw2_{gi}_{wb8}")
            def mm2(e, hb2=hb2, wb2=wb2, wb8=wb8, gn=gn, tn=tn):
                last = None
                for sl8 in range(8):
                    for tt in range(gn):
                        for hf in range(2):
                            last = e.matmul(
                                PB[tt * 2 + hf][:, :],
                                hb2[:, sl8 * tn + tt * 128: sl8 * tn + (tt + 1) * 128],
                                wb2[:, sl8 * 1024 + hf * 512: sl8 * 1024 + (hf + 1) * 512],
                                start=(wb8 == 0 and sl8 == 0),
                                stop=(wb8 == 11 and sl8 == 7))
                return last
            st("tensor", mm2, [ht2, wt2], [BT[i] for i in range(2 * gn)],
               tag=f"mm2_{gi}_{wb8}")
        for tt in range(gn):
            gb = gout[tt % 2]
            gtok = f"gout{tt % 2}"
            for hf in range(2):
                st("scalar", lambda e, tt=tt, hf=hf, gb=gb: e.activation(
                    gb[:, hf * 512:(hf + 1) * 512], PB[tt * 2 + hf][:, :], AF.Silu),
                   [BT[tt * 2 + hf]], [gtok], tag=f"gsilu{gi}_{tt}_{hf}")
            dma("sync", g_scr[(g0 + tt) * 128:(g0 + tt + 1) * 128, :], gb[:, :],
                [gtok], [f"gscr{g0 + tt}"], tag=f"stg{gi}_{tt}")

    if DEBUG_TAPS:
        dma("sync", dbg["g0"], g_scr[0:128, :], [f"gscr0"], ["_tap_g0"], tag="tap_g0")
        dma("sync", dbg["hrow0"], H_scr[0:128, :], [f"Hrow0"], ["_tap_h0"], tag="tap_h0")

    p.barrier()
    if DEBUG_PHASE == "p2":
        p.final_wait()
        return nc, p

    st("vector", lambda e: e.memset(S_run[:, :], 0.0), [], ["S_run"])
    cc_wait = [("cc", 1)]
    for r8 in range(8):
        p.step("sync", lambda e, r8=r8: e.dma_start(
            out=xmch_f[:, :], in_=ag_out.ap()[r8 * 128:(r8 + 1) * 128, :]),
            reads=["xmch_f"], writes=["xmch_f"], dma=True, tag=f"ldag{r8}",
            extra_waits=cc_wait if r8 == 0 else ())
        st("vector", lambda e, r8=r8: e.scalar_tensor_tensor(
            S_run[:, :], xmch_f[:, :], sel8_sb[:, r8:r8 + 1], S_run[:, :],
            ALU.mult, ALU.add),
           ["xmch_f", "sel8", "S_run"], ["S_run"], tag=f"hinit{r8}")

    dma("gpsimd", WoutT_sb[:, :].rearrange("p (s j) -> p s j", s=16),
        WoutT.rearrange("(s p) j -> p s j", p=128), [], ["WoutT_raw"], tag="ldwo")
    for sl in range(16):
        st("vector", lambda e, sl=sl: e.tensor_scalar_mul(
            WoutT_sb[:, sl * 1024:(sl + 1) * 1024],
            WoutT_sb[:, sl * 1024:(sl + 1) * 1024], nw_sb[:, sl:sl + 1]),
           ["WoutT_raw", "nw"], ["WoutT"], tag=f"wosc{sl}")

    if DEBUG_P3B < 2:
        p.final_wait()
        return nc, p

    for c in range(NCH):
        cc0 = 128 + c * 128
        rc0 = c * 128
        zcb = zc[c % 2]
        ztok = f"zc{c % 2}"
        dma("sync", zcb[:, :].rearrange("p (s t) -> p s t", s=16),
            z_scr[:, cc0:cc0 + 128].rearrange("(s p) t -> p s t", p=128),
            [], [ztok], tag=f"ldz{c}")
        gb3 = gbuf[c % 2]
        gtok3 = f"gbuf{c % 2}"
        dma("sync", gb3[:, :], g_scr[rc0:rc0 + 128, :], [], [gtok3], tag=f"ldg{c}")
        dma("sync", xres[:, :], xin[cc0:cc0 + 128, :], [], ["xres"], tag=f"ldxr{c}")

        chunk_rows(c)
        chunk_btld(c)
        chunk_exp(c, rows_sb[0:1, 0:128], "rows", A_bar, "A_bar", "abar")
        chunk_exp(c, rows_sb[0:1, 128:256], "rows", ebuf, "ebuf", "elc")
        st("vector", lambda e, cc0=cc0: e.tensor_mul(
            F1[:, :], C_fm[:, cc0:cc0 + 128], ebuf[:, :]),
           ["C_fm", "ebuf"], ["F1"], tag=f"f1_{c}")
        chunk_exp(c, rows_sb[0:1, 256:384], "rows", ebuf, "ebuf", "er8")
        st("vector", lambda e: e.tensor_mul(F2base[:, :], Btld[:, :], ebuf[:, :]),
           ["Btld", "ebuf"], ["F2base"], tag=f"f2b{c}")
        chunk_exp(c, rows_sb[0:1, 384:512], "rows", ebuf, "ebuf", "eg")
        st("vector", lambda e, cc0=cc0: e.tensor_mul(
            Cs[:, :], C_fm[:, cc0:cc0 + 128], ebuf[:, :]),
           ["C_fm", "ebuf"], ["Cs"], tag=f"csx{c}")
        chunk_bend(c)
        chunk_xmtok(c)

        if c == 0:
            tap("rows0", rows_sb[:, :], ["rows"])
            tap("abar0", A_bar[:, :], ["A_bar"])
            tap("f2b0", F2base[:, :], ["F2base"])
            tap("f10", F1[:, :], ["F1"])
            tap("cs0", Cs[:, :], ["Cs"])
            tap("z0", zcb[:, :], [ztok])
            tap("sr0", S_run[:, :], ["S_run"])
        for I in range(1, 16):
            st("vector", lambda e, I=I: e.tensor_scalar_mul(
                F2run[:, :], F2run[:, :], d8[:, I - 1:I]),
               ["F2run", "d8"], ["F2run"], tag=f"f2r{c}_{I}")
            st("gpsimd", lambda e, I=I: e.tensor_copy(
                F2run[:, (I - 1) * 8:I * 8], F2base[:, (I - 1) * 8:I * 8]),
               ["F2base", "F2run"], ["F2run"], tag=f"f2c{c}_{I}")
            st("tensor", lambda e, I=I: e.matmul(
                PB[6][:, I * 8:(I + 1) * 8], F2run[:, :], F1[:, I * 8:(I + 1) * 8],
                start=True, stop=True),
               ["F2run", "F1"], ["B6"], tag=f"wmm{c}_{I}")
        if DEBUG_P3B < 3:
            continue
        st("gpsimd", lambda e, cc0=cc0: e.tensor_mul(
            Zbuf[:, 0:128], C_fm[:, cc0:cc0 + 128], Btld[:, :]),
           ["C_fm", "Btld"], ["Zbuf"], tag=f"z0_{c}")
        for k9 in range(1, BLK):
            src = Ppp[(k9 - 1) % 2]
            dst = Ppp[k9 % 2]
            stok, dtok = f"Ppp{(k9 - 1) % 2}", f"Ppp{k9 % 2}"
            if k9 == 1:
                st("gpsimd", lambda e, dst=dst: e.tensor_copy(dst[:, :], A_bar[:, :]),
                   ["A_bar"], [dtok], tag=f"p1_{c}")
            else:
                st("gpsimd", lambda e, src=src, dst=dst: e.tensor_mul(
                    dst[:, 1:128], A_bar[:, 1:128], src[:, 0:127]),
                   ["A_bar", stok], [dtok], tag=f"p{k9}_{c}")
            st("gpsimd", lambda e, dst=dst, cc0=cc0: e.tensor_mul(
                zcw[:, :], C_fm[:, cc0:cc0 + 128], dst[:, :]),
               ["C_fm", dtok], ["zcw"], tag=f"zcw{k9}_{c}")
            st("gpsimd", lambda e, k9=k9: e.tensor_mul(
                Zbuf[:, k9 * 128 + k9:(k9 + 1) * 128],
                zcw[:, k9:128], Btld[:, 0:128 - k9]),
               ["zcw", "Btld"], ["Zbuf"], tag=f"z{k9}_{c}")
        for h2 in range(2):
            st("tensor", lambda e, h2=h2: e.matmul(
                PB[7][0:1, 0:512], onesc[:, :], Zbuf[:, h2 * 512:(h2 + 1) * 512],
                start=True, stop=True),
               ["Zbuf", "onesc"], ["B7"], tag=f"ckmm{c}_{h2}")
            st("vector", lambda e, h2=h2: e.tensor_copy(
                ck_sb[:, h2 * 512:(h2 + 1) * 512], PB[7][0:1, 0:512]),
               ["B7"], ["ck"], tag=f"ckev{c}_{h2}")
        for h2 in range(2):
            st("tensor", lambda e, h2=h2: e.matmul(
                PB[7][:, 0:512], ones1[:, :], ck_sb[0:1, h2 * 512:(h2 + 1) * 512],
                start=True, stop=True),
               ["ck", "ones1"], ["B7"], tag=f"ckbmm{c}_{h2}")
            st("vector", lambda e, h2=h2: e.tensor_copy(
                ckb[:, h2 * 512:(h2 + 1) * 512], PB[7][:, 0:512]),
               ["B7"], ["ckb"], tag=f"ckbev{c}_{h2}")
        st("vector", lambda e: e.tensor_mul(Wsb[:, :], PB[6][:, 0:128], maskoff_sb[:, :]),
           ["B6", "maskoff"], ["Wsb"], tag=f"wev{c}")
        for k9 in range(BLK):
            st("vector", lambda e, k9=k9: e.copy_predicated(
                Wsb[:, :], bandmask_sb[:, k9 * 128:(k9 + 1) * 128],
                ckb[:, k9 * 128:(k9 + 1) * 128]),
               ["bandmask", "ckb", "Wsb"], ["Wsb"], tag=f"band{c}_{k9}")

        if c == 0:
            tap("W0", Wsb[:, :], ["Wsb"])
            tap("ck0", ck_sb[:, :], ["ck"])
        if DEBUG_P3B < 4:
            continue
        st("vector", lambda e: e.tensor_copy(S_run_bf[:, :], S_run[:, :]),
           ["S_run"], ["S_run_bf"], tag=f"srbf{c}")
        def ymm(e):
            last = None
            for d16 in range(16):
                pb = PB[d16 // 4]
                off = (d16 % 4) * 128
                e.matmul(pb[:, off:off + 128],
                         S_run_bf[:, d16 * 128:(d16 + 1) * 128],
                         Cs[:, :], start=True, stop=False)
                last = e.matmul(pb[:, off:off + 128],
                                xm_tok[:, d16 * 128:(d16 + 1) * 128],
                                Wsb[:, :], start=False, stop=True)
            return last
        st("tensor", ymm, ["S_run_bf", "Cs", "xm_tok", "Wsb"], BT[0:4], tag=f"ymm{c}")
        for d16 in range(16):
            pb = PB[d16 // 4]
            off = (d16 % 4) * 128
            st("vector", lambda e, d16=d16, pb=pb, off=off, rc0=rc0:
               e.scalar_tensor_tensor(
                   y_fm[:, d16 * 128:(d16 + 1) * 128],
                   xm_fm[:, d16 * 2048 + rc0: d16 * 2048 + rc0 + 128],
                   dp_sb[:, d16:d16 + 1], pb[:, off:off + 128], ALU.mult, ALU.add),
               [f"xm{d16}", "dp", BT[d16 // 4]], ["y_fm"], tag=f"yev{c}_{d16}")
        def smm(e):
            last = None
            for q4 in range(4):
                last = e.matmul(PB[q4][:, :], Bendtok[:, :],
                                xm_tok[:, q4 * 512:(q4 + 1) * 512],
                                start=True, stop=True)
            return last
        st("tensor", smm, ["Bendtok", "xm_tok"], BT[0:4], tag=f"smm{c}")
        for q4 in range(4):
            st("vector", lambda e, q4=q4: e.scalar_tensor_tensor(
                S_run[:, q4 * 512:(q4 + 1) * 512], S_run[:, q4 * 512:(q4 + 1) * 512],
                dccol[:, :], PB[q4][:, :], ALU.mult, ALU.add),
               ["S_run", "dc", BT[q4]], ["S_run"], tag=f"sup{c}_{q4}")
        if c == 0:
            tap("y0", y_fm[:, :], ["y_fm"])
        if DEBUG_P3B < 5:
            continue
        st("scalar", lambda e: e.activation(xmch_f[:, :], y_fm[:, :], AF.Square),
           ["y_fm"], ["xmch_f"], tag=f"ysq{c}")
        def rmsmm(e):
            last = None
            for d16 in range(16):
                last = e.matmul(PB[6][0:1, 128:256], onesc[:, :],
                                xmch_f[:, d16 * 128:(d16 + 1) * 128],
                                start=(d16 == 0), stop=(d16 == 15))
            return last
        st("tensor", rmsmm, ["xmch_f", "onesc"], ["B6"], tag=f"rmsmm{c}")
        st("scalar", lambda e: e.activation(rstdrow[:, :], PB[6][0:1, 128:256], AF.Ln,
                                            scale=1.0 / DI, bias=cbias[0:1, 0:1]),
           ["B6", "cbias"], ["rstdrow"], tag=f"rln{c}")
        st("scalar", lambda e: e.activation(rstdrow[:, :], rstdrow[:, :], AF.Exp,
                                            scale=-0.5),
           ["rstdrow"], ["rstdrow"], tag=f"rexp{c}")
        st("tensor", lambda e: e.matmul(PB[6][:, 384:385], rstdrow[:, :],
                                        onesc[0:1, 0:1], start=True, stop=True),
           ["rstdrow", "onesc"], ["B6"], tag=f"rtr{c}")
        st("vector", lambda e: e.tensor_copy(rstd_col[:, :], PB[6][:, 384:385]),
           ["B6"], ["rstd_col"], tag=f"rev{c}")
        if c == 0:
            tap("rstd0", rstd_col[:, :], ["rstd_col"])
        if DEBUG_P3B < 6:
            continue
        st("vector", lambda e, zcb=zcb: e.tensor_mul(yg[:, :], zcb[:, :], y_fm[:, :]),
           [ztok, "y_fm"], ["yg"], tag=f"gate{c}")
        def omm(e):
            last = None
            for d16 in range(16):
                for hf in range(2):
                    last = e.matmul(
                        PB[4 + hf][:, :], yg[:, d16 * 128:(d16 + 1) * 128],
                        WoutT_sb[:, d16 * 1024 + hf * 512: d16 * 1024 + (hf + 1) * 512],
                        start=(d16 == 0), stop=(d16 == 15))
            return last
        st("tensor", omm, ["yg", "WoutT"], [BT[4], BT[5]], tag=f"omm{c}")
        ob = outbuf[c % 2]
        otok = f"outbuf{c % 2}"
        if DEBUG_P3B < 7:
            continue
        for hf in range(2):
            st("vector", lambda e, hf=hf, gb3=gb3: e.scalar_tensor_tensor(
                sgbuf[:, hf * 512:(hf + 1) * 512], PB[4 + hf][:, :], rstd_col[:, :],
                gb3[:, hf * 512:(hf + 1) * 512], ALU.mult, ALU.mult),
               [BT[4 + hf], "rstd_col", gtok3], ["yg"], tag=f"sg{c}_{hf}")
        if DEBUG_P3B < 8:
            continue
        st("vector", lambda e, ob=ob: e.tensor_add(ob[:, :], sgbuf[:, :], xres[:, :]),
           ["yg", "xres"], [otok], tag=f"fin{c}")
        dma("sync", out[rc0:rc0 + 128, :], ob[:, :], [otok], [f"out{c}"],
            tag=f"stout{c}")

    if DEBUG_P3B < 99:
        p.final_wait()
        return nc, p
    p.final_wait()
    return nc, p


_nc_cache = {}


def _get_program():
    if "nc" not in _nc_cache:
        nc, p = build_program()
        with nc.Block() as block, \
             nc.semaphore("s_t") as s_t, nc.semaphore("s_v") as s_v, \
             nc.semaphore("s_s") as s_s, nc.semaphore("s_g") as s_g, \
             nc.semaphore("s_y") as s_y, nc.semaphore("s_cc") as s_cc:
            p.emit(nc, block, {"tensor": s_t, "vector": s_v, "scalar": s_s,
                               "gpsimd": s_g, "sync": s_y, "cc": s_cc})
        _nc_cache["nc"] = nc
    return _nc_cache["nc"]


def _host_inputs(L0_attn_x, L0_attn_in_proj_w, L0_attn_conv_w, L0_attn_A_log,
                 L0_attn_Dp, L0_attn_norm_w, L0_attn_out_proj_w,
                 L0_ffn_router_w, L0_ffn_w1, L0_ffn_w2):
    f32 = np.float32
    x = np.ascontiguousarray(np.asarray(L0_attn_x, dtype=f32))
    Win = np.ascontiguousarray(np.asarray(L0_attn_in_proj_w, dtype=f32))
    WinT = np.ascontiguousarray(Win.T)
    w1 = np.ascontiguousarray(np.asarray(L0_ffn_w1, dtype=f32))
    w2 = np.ascontiguousarray(np.asarray(L0_ffn_w2, dtype=f32))
    WoutT = np.ascontiguousarray(np.asarray(L0_attn_out_proj_w, dtype=f32).T)
    small9 = np.ascontiguousarray(
        np.concatenate([np.asarray(L0_ffn_router_w, f32).T,
                        Win[4352:4353, :].T], axis=1))
    A_log_row = np.ascontiguousarray(np.asarray(L0_attn_A_log, f32).reshape(1, S))
    cw = np.asarray(L0_attn_conv_w, f32).reshape(DI, 4)
    cw_col = np.ascontiguousarray(
        cw.reshape(16, 128, 4).transpose(1, 0, 2).reshape(128, 64))
    dp_col = np.ascontiguousarray(np.asarray(L0_attn_Dp, f32).reshape(16, 128).T)
    nw_col = np.ascontiguousarray(np.asarray(L0_attn_norm_w, f32).reshape(16, 128).T)

    tt = np.arange(128)
    M8 = ((tt[:, None] // BLK == tt[None, :] // BLK) & (tt[:, None] <= tt[None, :]))
    M8r = ((tt[:, None] // BLK == tt[None, :] // BLK) & (tt[:, None] > tt[None, :]))
    Mlow = tt[:, None] <= tt[None, :]
    Mrows = np.ascontiguousarray(
        np.concatenate([np.eye(128), M8, M8r, Mlow], axis=1).astype(f32))
    maskoff = np.ascontiguousarray(
        (tt[:, None] // BLK < tt[None, :] // BLK).astype(f32))
    bandmask = np.ascontiguousarray(np.concatenate(
        [((tt[None, :] - tt[:, None] == k) & (tt[:, None] // BLK == tt[None, :] // BLK))
         for k in range(BLK)], axis=1).astype(np.uint8))
    idf32 = np.eye(128, dtype=f32)
    eyerow8 = np.zeros((8, 8 * 128), f32)
    for e in range(8):
        eyerow8[e, e * 128:(e + 1) * 128] = 1.0

    shared = dict(WinT=WinT, w1=w1, w2=w2, WoutT=WoutT, small9=small9,
                  A_log_row=A_log_row, cw_col=cw_col, dp_col=dp_col,
                  nw_col=nw_col, Mrows=Mrows, maskoff=maskoff,
                  bandmask=bandmask, idf32=idf32, eyerow8=eyerow8)

    in_maps = []
    for core in range(8):
        b, h = core // 2, core % 2
        xin = np.zeros((TW, DM), f32)
        if h == 1:
            xin[125:128] = x[b, T - 3:T]
        xin[128:] = x[b, h * T:(h + 1) * T]
        sel = np.zeros((128, 8), f32)
        if h == 1:
            sel[:, 2 * b] = 1.0
        m = dict(shared)
        m["xin"] = np.ascontiguousarray(xin)
        m["sel8"] = sel
        in_maps.append(m)
    return in_maps


_LAST_RES = None


def kernel(**inputs):
    global _LAST_RES
    nc = _get_program()
    in_maps = _host_inputs(**inputs)
    res = run_bass_kernel_spmd(nc, in_maps, core_ids=list(range(8)))
    _LAST_RES = res
    outp = np.empty((4, 4096, 1024), np.float32)
    for core in range(8):
        b, h = core // 2, core % 2
        outp[b, h * T:(h + 1) * T] = np.asarray(res.results[core]["out"])
    return outp
